# version 12
# baseline (speedup 1.0000x reference)
import contextlib
import numpy as np
import ml_dtypes
import concourse.bass as bass
import concourse.mybir as mybir
from concourse.bass_utils import run_bass_kernel_spmd

F32 = mybir.dt.float32
BF16 = mybir.dt.bfloat16
AF = mybir.ActivationFunctionType
ALU = mybir.AluOpType

P = 128
TT = 512
NDC = 16
NHC = 64
NH = 16
CH = 128
NCHUNK = TT // CH
D = 2048
HID = 8192
SCALE = float(128 ** -0.5)

ENGINES = ("sync", "scalar", "vector", "tensor", "gpsimd")

CST_IDENT, CST_H128 = 0, 1
CST_MX1, CST_MX2, CST_MX4, CST_MX8, CST_MX8S = 2, 3, 4, 5, 6
CST_TX1, CST_TX2, CST_TX4, CST_TX8, CST_TX8S = 7, 8, 9, 10, 11
N_CST = 12

HC_ONB = 0
HC_LAD = 8192
HC_QH = 23552
HC_KH = 24576
HC_VH = 25600
HC_GH = 26624
HC_GSIL = 27648
HC_EXW = 28672
HC_CSC = 29696


def _np_consts():
    t = np.arange(CH)
    tc, sc = t[:, None], t[None, :]

    def lvl(m):
        return ((tc // m == sc // m + 1) & (tc // (2 * m) == sc // (2 * m)))

    m_x1 = ((tc == sc + 1) & (tc % 2 == 1)).astype(np.float32)
    m_x2 = lvl(2).astype(np.float32)
    m_x4 = lvl(4).astype(np.float32)
    m_x8 = lvl(8).astype(np.float32)
    blk8 = t // 8
    m_x8s = ((blk8[:, None] == blk8[None, :] + 1)
             & (blk8[:, None] % 2 == 0)).astype(np.float32)

    ident = np.eye(P, dtype=np.float32)
    h = np.array([[1.0]], dtype=np.float32)
    while h.shape[0] < P:
        h = np.block([[h, h], [h, -h]])

    tiles = [ident, h, m_x1, m_x2, m_x4, m_x8, m_x8s,
             m_x1.T, m_x2.T, m_x4.T, m_x8.T, m_x8s.T]
    return np.stack(tiles).astype(ml_dtypes.bfloat16)


class Bld:
    def __init__(self, nc):
        self.nc = nc
        self.streams = {e: [] for e in ENGINES}
        self.counts = {}
        self.sems = {}
        self.bank_last = {}
        self.last_wait = {}

    def sem(self, stack, name):
        h = stack.enter_context(self.nc.semaphore(name))
        self.sems[name] = h
        self.counts[name] = 0
        return h

    def op(self, eng, fn, inc=None, n=1):
        if inc is not None:
            sem = self.sems[inc]

            def wrapped(e, fn=fn, sem=sem, n=n):
                fn(e).then_inc(sem, n)
            self.streams[eng].append(wrapped)
            self.counts[inc] += n
            return self.counts[inc]
        self.streams[eng].append(fn)
        return None

    def wait(self, eng, semname, val):
        if not val:
            return
        key = (eng, semname)
        if self.last_wait.get(key, 0) >= val:
            return
        self.last_wait[key] = val
        sem = self.sems[semname]
        self.streams[eng].append(
            lambda e, sem=sem, val=val: e.wait_ge(sem, val))

    def touch(self, bank, eng, count):
        self.bank_last.setdefault(bank, {})[eng] = count

    def guard(self, bank, eng):
        sname = "s_" + eng
        for oeng, cnt in self.bank_last.get(bank, {}).items():
            if oeng != eng:
                self.wait(eng, "s_" + oeng, cnt)


def build_nc(ntile=8, dbg=None):
    nc = bass.Bass("TRN2", target_bir_lowering=False, debug=False)
    T_ALL = ntile * TT

    def din(name, shape, dtype=F32):
        return nc.dram_tensor(name, shape, dtype, kind="ExternalInput").ap()

    xT = din("xT", [D, T_ALL])
    w1tf = din("w1tf", [D, HID]); w3tf = din("w3tf", [D, HID])
    w2tf = din("w2tf", [HID, D])
    wqf = din("wq", [D, D]); wkf = din("wk", [D, D]); wvf = din("wv", [D, D])
    wgf = din("wg", [D, D]); wof = din("wo", [D, D])
    wf1 = din("wf1", [D, P]); wf2 = din("wf2", [P, D])
    wbf = din("wb", [D, NH])
    convq = din("convq", [D, 4]); convk = din("convk", [D, 4])
    convv = din("convv", [D, 4])
    alog = din("alog", [NH]); dtb = din("dtb", [D]); bgv = din("bgv", [D])
    onw = din("onw", [P])
    cst = din("cst", [N_CST, P, P], BF16)

    outT = nc.dram_tensor("outT", [D, T_ALL], F32, kind="ExternalOutput").ap()

    def scr(name, shape):
        return nc.dram_tensor(name, shape, BF16, kind="Internal").ap()

    w1b = scr("w1b", [NHC, P, D]); w3b = scr("w3b", [NHC, P, D])
    wqb = scr("wqb", [NDC, P, D]); wkb = scr("wkb", [NDC, P, D])
    wvb = scr("wvb", [NDC, P, D]); wgb = scr("wgb", [NDC, P, D])
    wob = scr("wob", [NDC, P, D])
    zscr = scr("zscr", [NHC, P, D])
    w2fb = scr("w2fb", [NDC, P, HID])

    b = Bld(nc)

    with contextlib.ExitStack() as st:
        def sb(name, shape, dtype):
            return st.enter_context(nc.sbuf_tensor(name, shape, dtype))

        def psum(name, shape, dtype):
            return st.enter_context(nc.psum_tensor(name, shape, dtype))

        for sname in ("s_tensor", "s_vector", "s_scalar", "s_gpsimd",
                      "s_dmaw", "s_wa0", "s_wa1", "s_wb0", "s_wb1",
                      "s_wc0", "s_wc1", "s_wcq", "s_wck", "s_wcv",
                      "s_dmax", "s_st0", "s_st1", "s_stw", "s_dmw2"):
            b.sem(st, sname)

        xres = sb("xres", [P, NDC * TT], F32)
        xs = sb("xs", [P, NDC * TT], BF16)
        hid = sb("hid", [P, NHC * TT], BF16)
        wA = sb("wA", [P, 2 * D], BF16)
        wB = sb("wB", [P, 2 * D], BF16)
        wC = sb("wC", [P, HID], BF16)
        silb = sb("silb", [P, 3 * TT], BF16)
        ostg = sb("ostg", [P, 2 * TT], F32)
        rows = sb("rows", [1, 1792], F32)
        rowsb = sb("rowsb", [1, 2304], BF16)
        ones = sb("ones", [P, 1], BF16)
        onesr = sb("onesr", [1, P], BF16)
        scaler = sb("scaler", [1, P], BF16)
        cstt = sb("cstt", [P, N_CST * P], BF16)
        wf1b = sb("wf1b", [P, NDC * P], BF16)
        wf2b = sb("wf2b", [P, D], BF16)
        wbb = sb("wbb", [P, NDC * NH], BF16)
        convw = sb("convw", [P, 3 * NDC * 4], F32)
        dtbc = sb("dtbc", [P, NDC], F32)
        bgc = sb("bgc", [P, NDC], F32)
        onwc = sb("onwc", [P, 1], F32)
        eac = sb("eac", [P, NH], F32)
        cbias = sb("cbias", [P, 8], F32)
        betf = sb("betf", [NH, TT], BF16)
        bett = sb("bett", [P, NCHUNK * NH], F32)
        t1f = sb("t1f", [P, TT], BF16)
        fscr = sb("fscr", [P, 1808], F32)
        convt = sb("convt", [P, 3 * NH * 3], BF16)
        cgw = sb("cgw", [P, 2 * TT], F32)
        cglw = sb("cglw", [P, 2 * TT], F32)
        Sst = sb("Sst", [P, NH * CH], F32)
        Ssb = sb("Ssb", [P, 2 * CH], BF16)
        accf = sb("accf", [P, 4 * CH], F32)

        PS = [psum(f"ps{i}", [P, TT], F32) for i in range(6)]
        PS7P = [psum("ps6b", [P, 2 * TT], BF16), psum("ps7b", [P, 2 * TT], BF16)]

        FS_CH = [0, 136]
        FS_CONV = 272
        FS_G = 784
        RW_LNA, RW_LNB = 0, 512
        RW_L2P = [512, 1024]
        RW_RMS = [1536, 1664]
        RB_LN = 0
        RB_L2P = [1024, 1536]
        RB_RMS = [2048, 2176]

        def hsl(c0, w):
            return hid[:, c0:c0 + w]

        def lad(i, par):
            return hid[:, HC_LAD + i * 1024 + par * TT:
                       HC_LAD + i * 1024 + (par + 1) * TT]
        (L_KP1, L_QP1, L_KP2, L_QP2, L_KM2, L_KP4, L_QP4, L_KM4,
         L_KP8, L_QP8, L_KM8, L_KM16, L_KAP, L_QTL, L_KBR) = range(15)

        def csl(par, i):
            c0 = HC_CSC + (par * 12 + i) * CH
            return hid[:, c0:c0 + CH]

        ident = cstt[:, CST_IDENT * P:(CST_IDENT + 1) * P]
        h128 = cstt[:, CST_H128 * P:(CST_H128 + 1) * P]

        def mask(i):
            return cstt[:, i * P:(i + 1) * P]

        ncst = b.op("sync", lambda e: e.dma_start(
            out=cstt[:].rearrange("p (a q) -> p a q", q=P),
            in_=cst.rearrange("a p q -> p a q")), inc="s_dmaw", n=16)
        b.op("gpsimd", lambda e: e.memset(ones[:], 1.0))
        b.op("gpsimd", lambda e: e.memset(onesr[:], 1.0))
        b.op("gpsimd", lambda e: e.memset(scaler[:], SCALE))
        b.op("gpsimd", lambda e: e.memset(convt[:], 0.0))
        b.op("gpsimd", lambda e: e.memset(cbias[:, 0:1], 1e-5))
        b.op("gpsimd", lambda e: e.memset(cbias[:, 1:2], 1e-6))
        b.op("gpsimd", lambda e: e.memset(cbias[:, 2:3], 1.0))
        b.op("gpsimd", lambda e: e.memset(cbias[:, 3:4], 40.0))
        b.op("gpsimd", lambda e: e.memset(cbias[:, 4:5], -40.0))
        b.op("gpsimd", lambda e: e.memset(Sst[:], 0.0))
        ng = b.op("gpsimd", lambda e: e.memset(silb[:, 0:TT + 8], 0.0),
                  inc="s_gpsimd")

        for name_, src, dst_sl in (
                ("dtb", dtb, dtbc), ("bgv", bgv, bgc)):
            b.op("sync", lambda e, src=src, dst_sl=dst_sl: e.dma_start(
                out=dst_sl[:, 0:NDC], in_=src.rearrange("(a p) -> p a", p=P),
                allow_slow_non_contiguous=True), inc="s_dmaw", n=16)
        b.op("sync", lambda e: e.dma_start(
            out=onwc[:], in_=onw.rearrange("(p one) -> p one", one=1),
            allow_slow_non_contiguous=True), inc="s_dmaw", n=16)
        b.op("sync", lambda e: e.dma_start(
            out=rows[0:1, 0:NH], in_=alog.rearrange("(one h) -> one h", one=1)),
            inc="s_dmaw", n=16)
        for i, cw in enumerate((convq, convk, convv)):
            b.op("sync", lambda e, cw=cw, i=i: e.dma_start(
                out=convw[:, i * 64:(i + 1) * 64]
                    .rearrange("p (a c) -> p a c", c=4),
                in_=cw.rearrange("(a p) c -> p a c", p=P),
                allow_slow_non_contiguous=True), inc="s_dmaw", n=16)
        nsm = b.op("sync", lambda e: e.dma_start(
            out=fscr[:, 0:NDC * NH].rearrange("p (a c) -> p a c", c=NH),
            in_=wbf.rearrange("(a p) c -> p a c", p=P),
            allow_slow_non_contiguous=True), inc="s_dmaw", n=16)
        b.wait("vector", "s_dmaw", nsm)
        nv = b.op("vector", lambda e: e.tensor_copy(
            out=wbb[:], in_=fscr[:, 0:NDC * NH]), inc="s_vector")
        b.wait("scalar", "s_dmaw", b.counts["s_dmaw"])
        na = b.op("scalar", lambda e: e.activation(
            out=rows[0:1, 64:64 + NH], in_=rows[0:1, 0:NH], func=AF.Exp),
            inc="s_scalar")
        b.wait("vector", "s_scalar", na)
        nv = b.op("vector", lambda e: e.tensor_scalar_mul(
            out=rowsb[0:1, 0:NH], in0=rows[0:1, 64:64 + NH], scalar1=-1.0),
            inc="s_vector")
        b.wait("tensor", "s_vector", nv)
        b.wait("tensor", "s_dmaw", b.counts["s_dmaw"])
        nt = b.op("tensor", lambda e: e.matmul(
            PS[0][:, 0:NH], onesr[:], rowsb[0:1, 0:NH], start=True, stop=True),
            inc="s_tensor")
        b.touch("ps0", "tensor", nt)
        b.wait("vector", "s_tensor", nt)
        nv = b.op("vector", lambda e: e.tensor_copy(
            out=eac[:], in_=PS[0][:, 0:NH]), inc="s_vector")
        b.touch("ps0", "vector", nv)

        w = b.op("sync", lambda e: e.dma_start(
            out=xres[:, 0:NDC * P].rearrange("p (a h) -> p a h", h=P),
            in_=wf1.rearrange("(a p) h -> p a h", p=P),
            allow_slow_non_contiguous=True), inc="s_dmaw", n=16)
        b.wait("vector", "s_dmaw", b.counts["s_dmaw"])
        nv = b.op("vector", lambda e: e.tensor_copy(
            out=wf1b[:], in_=xres[:, 0:NDC * P]), inc="s_vector")
        b.wait("sync", "s_vector", nv)
        w = b.op("sync", lambda e: e.dma_start(
            out=xres[:, 0:D], in_=wf2[:, :]), inc="s_dmaw", n=16)
        b.wait("vector", "s_dmaw", b.counts["s_dmaw"])
        nv = b.op("vector", lambda e: e.tensor_copy(
            out=wf2b[:], in_=xres[:, 0:D]), inc="s_vector")
        b.wait("sync", "s_vector", nv)

        PJW = 4096
        jobs = []

        def blk_store(dst, gb, gw, dc):
            def go(e, hh, og):
                return e.dma_start(
                    out=dst[gb:gb + gw, :, dc * P:(dc + 1) * P]
                        .rearrange("a p h -> p a h"),
                    in_=hh[:, og * P:(og + gw) * P]
                        .rearrange("p (a h) -> p a h", h=P))
            return go

        def add_weight(src, dst, ncols):
            nhalf = (ncols + PJW - 1) // PJW
            for dc in range(NDC):
                for half in range(nhalf):
                    c0 = half * PJW
                    cw = min(PJW, ncols - c0)
                    stores = []
                    for og in range(0, cw // P, 8):
                        gw = min(8, cw // P - og)
                        gb = c0 // P + og
                        stores.append((blk_store(dst, gb, gw, dc), og))
                    jobs.append((src[dc * P:(dc + 1) * P, c0:c0 + cw],
                                 cw, stores))

        add_weight(w1tf, w1b, HID)
        add_weight(w3tf, w3b, HID)
        add_weight(wqf, wqb, D)
        add_weight(wkf, wkb, D)
        add_weight(wvf, wvb, D)
        add_weight(wgf, wgb, D)
        add_weight(wof, wob, D)

        cast_done = {}
        store_done = {}
        w = 0
        for i, (src, cw, stores) in enumerate(jobs):
            buf = i % 2
            qeng, qsem = ("sync", "s_dmaw") if buf == 0 else ("scalar", "s_dmw2")
            xsl_ = xres[:, buf * PJW:buf * PJW + cw]
            hhp = hid[:, buf * PJW:buf * PJW + cw]
            if i >= 2:
                b.wait(qeng, "s_vector", cast_done[i - 2])
            ld = b.op(qeng, lambda e, xsl_=xsl_, src=src: e.dma_start(
                out=xsl_, in_=src), inc=qsem, n=16)
            b.wait("vector", qsem, ld)
            if i >= 2:
                b.wait("vector", "s_stw", store_done[i - 2])
            cv = b.op("vector", lambda e, hhp=hhp, xsl_=xsl_: e.tensor_copy(
                out=hhp, in_=xsl_), inc="s_vector")
            cast_done[i] = cv
            b.wait("gpsimd", "s_vector", cv)
            for st, og in stores:
                w = b.op("gpsimd", lambda e, st=st, hhp=hhp, og=og: st(e, hhp, og),
                         inc="s_stw", n=16)
            store_done[i] = w

        SC = float(1.0 / np.sqrt(8192.0))

        def slotap(s):
            if s < 16:
                return hid[:, s * D:(s + 1) * D]
            if s < 18:
                return wA[:, (s - 16) * D:(s - 15) * D]
            return wB[:, (s - 18) * D:(s - 17) * D]

        SLOTSETS = [list(range(10)), list(range(10, 20))]
        jobs_stw = b.counts["s_stw"]
        last_job_cast = b.counts["s_vector"]
        slotfree = {0: jobs_stw, 1: jobs_stw}
        slotread = {0: 0, 1: 0}
        slot_mm = {0: 0, 1: 0}
        q_cast = {q: last_job_cast for q in range(4)}
        gq = 0
        pass0_last_store = 0
        for gpass in range(2):
            for g in range(8):
                idxs = ([g * 8 + j for j in range(8)] if gpass == 0
                        else [g + 8 * j for j in range(8)])
                par = g % 2
                slots0 = SLOTSETS[par]
                if gpass == 0:
                    cv_j = []
                    for j, r in enumerate(idxs):
                        q = gq % 4
                        gq += 1
                        xq = xres[:, q * D:(q + 1) * D]
                        b.wait("sync", "s_vector", q_cast[q])
                        ld = b.op("sync", lambda e, xq=xq, r=r: e.dma_start(
                            out=xq, in_=w2tf[r * P:(r + 1) * P, :]),
                            inc="s_dmaw", n=16)
                        b.wait("vector", "s_dmaw", ld)
                        b.wait("vector", "s_stw", slotfree[par])
                        b.wait("vector", "s_tensor", slot_mm[par])
                        sj = slotap(slots0[j])
                        cv = b.op("vector", lambda e, sj=sj, xq=xq: e.tensor_copy(
                            out=sj, in_=xq), inc="s_vector")
                        q_cast[q] = cv
                        cv_j.append(cv)
                    for j in range(8):
                        sj = slotap(slots0[j])
                        b.wait("tensor", "s_vector", cv_j[j])
                        for dsl in range(4):
                            pb_ = dsl % 2
                            b.guard(f"ps{pb_}", "tensor")
                            nt = b.op("tensor", lambda e, sj=sj, dsl=dsl, pb_=pb_: e.matmul(
                                PS[pb_][:, :], h128,
                                sj[:, dsl * TT:(dsl + 1) * TT],
                                start=True, stop=True), inc="s_tensor")
                            b.touch(f"ps{pb_}", "tensor", nt)
                            b.wait("vector", "s_tensor", nt)
                            nv = b.op("vector", lambda e, sj=sj, dsl=dsl, pb_=pb_: e.tensor_scalar_mul(
                                out=sj[:, dsl * TT:(dsl + 1) * TT],
                                in0=PS[pb_][:, :], scalar1=SC), inc="s_vector")
                            b.touch(f"ps{pb_}", "vector", nv)
                            b.wait("tensor", "s_vector", nv)
                    slot_mm[par] = b.counts["s_tensor"]
                else:
                    if g < 2:
                        b.wait("sync", "s_stw", pass0_last_store)
                    b.wait("sync", "s_stw", slotfree[par])
                    b.wait("sync", "s_vector", slotread[par])
                    lds = 0
                    for j, r in enumerate(idxs):
                        sj = slotap(slots0[j])
                        lds = b.op("sync", lambda e, sj=sj, r=r: e.dma_start(
                            out=sj, in_=zscr[r, :, :]), inc="s_dmaw", n=16)
                    b.wait("vector", "s_dmaw", lds)
                slot = list(slots0[:8])
                spare = [slots0[8], slots0[9]]
                for stg in range(3):
                    h_ = 1 << stg
                    new_slot = [0] * 8
                    for a in range(8):
                        if (a >> stg) & 1:
                            continue
                        bb_ = a + h_
                        sa, sbx = slot[a], slot[bb_]
                        da, db = spare
                        nv = b.op("vector", lambda e, sa=sa, sbx=sbx, da=da: e.tensor_add(
                            out=slotap(da), in0=slotap(sa), in1=slotap(sbx)),
                            inc="s_vector")
                        nv = b.op("vector", lambda e, sa=sa, sbx=sbx, db=db: e.tensor_sub(
                            out=slotap(db), in0=slotap(sa), in1=slotap(sbx)),
                            inc="s_vector")
                        new_slot[a], new_slot[bb_] = da, db
                        spare = [sa, sbx]
                    slot = new_slot
                slotread[par] = nv
                b.wait("gpsimd", "s_vector", nv)
                for j, r in enumerate(idxs):
                    sj = slotap(slot[j])
                    if gpass == 0:
                        w = b.op("gpsimd", lambda e, r=r, sj=sj: e.dma_start(
                            out=zscr[r, :, :], in_=sj), inc="s_stw", n=16)
                    else:
                        w = b.op("gpsimd", lambda e, r=r, sj=sj: e.dma_start(
                            out=w2fb[:, :, r * P:(r + 1) * P]
                                .rearrange("a p h -> p a h"),
                            in_=sj.rearrange("p (a h) -> p a h", h=P)),
                            inc="s_stw", n=16)
                slotfree[par] = w
                if gpass == 0 and g == 7:
                    pass0_last_store = w
        prep_dma = b.counts["s_dmaw"]
        prep_stw = b.counts["s_stw"]
        prep_vec = b.counts["s_vector"]


        def ln_pass(src_sem=None, src_val=0):
            xb = lambda dc: hid[:, dc * TT:(dc + 1) * TT]
            sq = lambda dc: hid[:, (NDC + dc) * TT:(NDC + dc + 1) * TT]
            if src_sem is not None:
                b.wait("scalar", src_sem, src_val)
            cvl = []
            sql = []
            for dc in range(NDC):
                nv = b.op("vector", lambda e, dc=dc: e.tensor_copy(
                    out=xb(dc), in_=xres[:, dc * TT:(dc + 1) * TT]),
                    inc="s_vector")
                cvl.append(nv)
                na = b.op("scalar", lambda e, dc=dc: e.activation(
                    out=sq(dc), in_=xres[:, dc * TT:(dc + 1) * TT],
                    func=AF.Square), inc="s_scalar")
                sql.append(na)
            b.guard("ps5", "tensor")
            b.guard("ps4", "tensor")
            for dc in range(NDC):
                b.wait("tensor", "s_vector", cvl[dc])
                nt = b.op("tensor", lambda e, dc=dc: e.matmul(
                    PS[5][0:1, 0:TT], ones[:], xb(dc),
                    start=(dc == 0), stop=(dc == NDC - 1)),
                    inc="s_tensor" if dc == NDC - 1 else None)
                if dc == NDC - 1:
                    nt_sum = nt
                b.wait("tensor", "s_scalar", sql[dc])
                nt = b.op("tensor", lambda e, dc=dc: e.matmul(
                    PS[4][0:1, 0:TT], ones[:], sq(dc),
                    start=(dc == 0), stop=(dc == NDC - 1)),
                    inc="s_tensor" if dc == NDC - 1 else None)
                if dc == NDC - 1:
                    nt_sq = nt
            b.touch("ps5", "tensor", nt_sum)
            b.touch("ps4", "tensor", nt_sq)
            b.wait("scalar", "s_tensor", nt_sum)
            na = b.op("scalar", lambda e: e.activation(
                out=rows[0:1, RW_LNA:RW_LNA + TT], in_=PS[5][0:1, 0:TT],
                func=AF.Square, scale=1.0 / 2048.0), inc="s_scalar")
            b.touch("ps5", "scalar", na)
            b.wait("vector", "s_scalar", na)
            b.wait("vector", "s_tensor", nt_sq)
            nv = b.op("vector", lambda e: e.scalar_tensor_tensor(
                out=rows[0:1, RW_LNA:RW_LNA + TT], in0=PS[4][0:1, 0:TT],
                scalar=1.0 / 2048.0, in1=rows[0:1, RW_LNA:RW_LNA + TT],
                op0=ALU.mult, op1=ALU.subtract), inc="s_vector")
            b.touch("ps4", "vector", nv)
            b.wait("scalar", "s_vector", nv)
            na = b.op("scalar", lambda e: e.activation(
                out=rows[0:1, RW_LNA:RW_LNA + TT],
                in_=rows[0:1, RW_LNA:RW_LNA + TT], func=AF.Ln, bias=cbias[0:1, 0:1]),
                inc="s_scalar")
            na = b.op("scalar", lambda e: e.activation(
                out=rows[0:1, RW_LNA:RW_LNA + TT],
                in_=rows[0:1, RW_LNA:RW_LNA + TT], func=AF.Exp, scale=-0.5),
                inc="s_scalar")
            b.wait("vector", "s_scalar", na)
            nv = b.op("vector", lambda e: e.scalar_tensor_tensor(
                out=rows[0:1, RW_LNB:RW_LNB + TT], in0=PS[5][0:1, 0:TT],
                scalar=1.0 / 2048.0, in1=rows[0:1, RW_LNA:RW_LNA + TT],
                op0=ALU.mult, op1=ALU.mult), inc="s_vector")
            b.touch("ps5", "vector", nv)
            nv = b.op("vector", lambda e: e.tensor_copy(
                out=rowsb[0:1, RB_LN:RB_LN + TT],
                in_=rows[0:1, RW_LNA:RW_LNA + TT]), inc="s_vector")
            nv = b.op("vector", lambda e: e.tensor_copy(
                out=rowsb[0:1, RB_LN + TT:RB_LN + 2 * TT],
                in_=rows[0:1, RW_LNB:RW_LNB + TT]), inc="s_vector")
            b.wait("tensor", "s_vector", nv)
            b.guard("ps5", "tensor")
            b.guard("ps4", "tensor")
            nt = b.op("tensor", lambda e: e.matmul(
                PS[5][:, 0:TT], onesr[:], rowsb[0:1, RB_LN:RB_LN + TT],
                start=True, stop=True), inc="s_tensor")
            b.touch("ps5", "tensor", nt)
            nt = b.op("tensor", lambda e: e.matmul(
                PS[4][:, 0:TT], onesr[:], rowsb[0:1, RB_LN + TT:RB_LN + 2 * TT],
                start=True, stop=True), inc="s_tensor")
            b.touch("ps6", "tensor", nt)
            b.wait("vector", "s_tensor", nt)
            nv = b.op("vector", lambda e: e.tensor_copy(
                out=silb[:, 0:TT], in_=PS[4][:, 0:TT]), inc="s_vector")
            b.touch("ps4", "vector", nv)
            xs_v = []
            for dc in range(NDC):
                nv = b.op("vector", lambda e, dc=dc: e.tensor_mul(
                    out=sq(dc), in0=xb(dc), in1=PS[5][:, 0:TT]), inc="s_vector")
                b.touch("ps5", "vector", nv)
                nv = b.op("vector", lambda e, dc=dc: e.tensor_sub(
                    out=xs[:, dc * TT:(dc + 1) * TT], in0=sq(dc),
                    in1=silb[:, 0:TT]), inc="s_vector")
                xs_v.append(nv)
            ng = b.op("gpsimd", lambda e: e.engine_nop(), inc="s_gpsimd")
            b.wait("gpsimd", "s_vector", nv)
            ng = b.op("gpsimd", lambda e: e.engine_nop(), inc="s_gpsimd")
            return ng, xs_v

        def ffn_pass(it, first, xs_v=None):
            xs_t = lambda dc: xs[:, dc * TT:(dc + 1) * TT]
            hid_t = lambda hc: hid[:, hc * TT:(hc + 1) * TT]
            up_pe = {}
            hid_ev = {}
            ent_v = b.counts["s_vector"]
            for hc in range(NHC):
                slot = hc % 2
                sema = "s_wa%d" % slot
                semb = "s_wb%d" % slot
                if hc < 2:
                    b.wait("sync", "s_tensor", slot_last["wa%d" % slot])
                    b.wait("sync", "s_tensor", slot_last["wb%d" % slot])
                if hc >= 2:
                    b.wait("sync", "s_tensor", up_pe[hc - 2])
                wa = b.op("sync", lambda e, hc=hc, slot=slot: e.dma_start(
                    out=wA[:, slot * D:(slot + 1) * D], in_=w1b[hc]),
                    inc=sema, n=16)
                wb_ = b.op("sync", lambda e, hc=hc, slot=slot: e.dma_start(
                    out=wB[:, slot * D:(slot + 1) * D], in_=w3b[hc]),
                    inc=semb, n=16)
                b.wait("tensor", sema, wa)
                b.wait("tensor", semb, wb_)
                if hc >= 2:
                    b.wait("tensor", "s_vector", hid_ev[hc - 2])
                pu1, pu3 = PS[0 + slot], PS[2 + slot]
                b.guard(f"ps{0 + slot}", "tensor")
                b.guard(f"ps{2 + slot}", "tensor")
                for dc in range(NDC):
                    if hc == 0 and xs_v is not None:
                        b.wait("tensor", "s_vector", xs_v[dc])
                    nt = b.op("tensor", lambda e, dc=dc, slot=slot, pu1=pu1: e.matmul(
                        pu1[:, :],
                        wA[:, (slot * NDC + dc) * P:(slot * NDC + dc + 1) * P],
                        xs_t(dc), start=(dc == 0), stop=(dc == NDC - 1)),
                        inc="s_tensor" if dc == NDC - 1 else None)
                nt1 = nt
                b.touch(f"ps{0 + slot}", "tensor", nt1)
                slot_last["wa%d" % slot] = nt1
                for dc in range(NDC):
                    nt = b.op("tensor", lambda e, dc=dc, slot=slot, pu3=pu3: e.matmul(
                        pu3[:, :],
                        wB[:, (slot * NDC + dc) * P:(slot * NDC + dc + 1) * P],
                        xs_t(dc), start=(dc == 0), stop=(dc == NDC - 1)),
                        inc="s_tensor" if dc == NDC - 1 else None)
                nt3 = nt
                b.touch(f"ps{2 + slot}", "tensor", nt3)
                slot_last["wb%d" % slot] = nt3
                up_pe[hc] = nt3
                b.wait("scalar", "s_tensor", nt1)
                if hc >= 2:
                    b.wait("scalar", "s_vector", hid_ev[hc - 2])
                na = b.op("scalar", lambda e, slot=slot, pu1=pu1: e.activation(
                    out=silb[:, slot * TT:(slot + 1) * TT], in_=pu1[:, :],
                    func=AF.Silu), inc="s_scalar")
                b.touch(f"ps{0 + slot}", "scalar", na)
                b.wait("vector", "s_scalar", na)
                b.wait("vector", "s_tensor", nt3)
                nv = b.op("vector", lambda e, hc=hc, slot=slot, pu3=pu3: e.tensor_mul(
                    out=hid_t(hc), in0=silb[:, slot * TT:(slot + 1) * TT],
                    in1=pu3[:, :]), inc="s_vector")
                b.touch(f"ps{2 + slot}", "vector", nv)
                hid_ev[hc] = nv
            dn_pe = {}
            dn_ev = {}
            st_dma = {}
            for dt in range(NDC):
                slot = dt % 2
                if dt >= 2:
                    b.wait("sync", "s_tensor", dn_pe[dt - 2])
                if slot == 0:
                    if dt == 0:
                        b.wait("sync", "s_vector", ent_v)
                        for k_ in ("wc0", "wcq", "wck", "wcv"):
                            b.wait("sync", "s_tensor", slot_last[k_])
                    wl = b.op("sync", lambda e, dt=dt: e.dma_start(
                        out=wC[:, :], in_=w2fb[dt]), inc="s_wc0", n=16)
                    b.wait("tensor", "s_wc0", wl)
                else:
                    if dt == 1:
                        for k_ in ("wa0", "wa1", "wb0", "wb1"):
                            b.wait("sync", "s_tensor", slot_last[k_])
                    wl = b.op("sync", lambda e, dt=dt: e.dma_start(
                        out=wA[:, :], in_=w2fb[dt, :, 0:NHC * P // 2]),
                        inc="s_wa0", n=16)
                    wl2 = b.op("sync", lambda e, dt=dt: e.dma_start(
                        out=wB[:, :], in_=w2fb[dt, :, NHC * P // 2:]),
                        inc="s_wb0", n=16)
                    b.wait("tensor", "s_wa0", wl)
                    b.wait("tensor", "s_wb0", wl2)
                if dt == 0:
                    b.wait("tensor", "s_vector", hid_ev[NHC - 1])
                pd = PS[4 + slot]
                b.guard(f"ps{4 + slot}", "tensor")
                if dt >= 2:
                    b.wait("tensor", "s_vector", dn_ev[dt - 2])
                    if not first:
                        b.wait("tensor", "s_st%d" % slot, st_dma[dt - 2])
                for hc in range(NHC):
                    if slot == 0:
                        lhs = wC[:, hc * P:(hc + 1) * P]
                    elif hc < 32:
                        lhs = wA[:, hc * P:(hc + 1) * P]
                    else:
                        lhs = wB[:, (hc - 32) * P:(hc - 31) * P]
                    nt = b.op("tensor", lambda e, hc=hc, pd=pd, lhs=lhs: e.matmul(
                        pd[:, :], lhs, hid_t(hc),
                        start=(hc == 0), stop=(hc == NHC - 1)),
                        inc="s_tensor" if hc == NHC - 1 else None)
                dn_pe[dt] = nt
                if slot == 1:
                    slot_last["wa0"] = nt
                    slot_last["wa1"] = nt
                    slot_last["wb0"] = nt
                    slot_last["wb1"] = nt
                else:
                    slot_last["wc0"] = nt
                b.touch(f"ps{4 + slot}", "tensor", nt)
                b.wait("vector", "s_tensor", nt)
                if first:
                    nv = b.op("vector", lambda e, dt=dt, pd=pd: e.scalar_tensor_tensor(
                        out=xres[:, dt * TT:(dt + 1) * TT], in0=pd[:, :],
                        scalar=0.5, in1=xres[:, dt * TT:(dt + 1) * TT],
                        op0=ALU.mult, op1=ALU.add), inc="s_vector")
                else:
                    nv = b.op("vector", lambda e, dt=dt, pd=pd, slot=slot: e.scalar_tensor_tensor(
                        out=ostg[:, slot * TT:(slot + 1) * TT], in0=pd[:, :],
                        scalar=0.5, in1=xres[:, dt * TT:(dt + 1) * TT],
                        op0=ALU.mult, op1=ALU.add), inc="s_vector")
                    b.wait("gpsimd", "s_vector", nv)
                    wst = b.op("gpsimd", lambda e, dt=dt, it=it, slot=slot: e.dma_start(
                        out=outT[dt * P:(dt + 1) * P, it * TT:(it + 1) * TT],
                        in_=ostg[:, slot * TT:(slot + 1) * TT]),
                        inc="s_st%d" % slot, n=16)
                    st_dma[dt] = wst
                b.touch(f"ps{4 + slot}", "vector", nv)
                dn_ev[dt] = nv
            return dn_ev[NDC - 1]

        snap = [dict(t=0, v=0, s=0, g=0), dict(t=0, v=0, s=0, g=0)]
        conv_read = {"n": 0}
        slot_last = {"wa0": 0, "wa1": 0, "wb0": 0, "wb1": 0,
                     "wcq": 0, "wck": 0, "wcv": 0, "wc0": 0}
        l2_last = {0: 0, 1: 0}
        t1f_done_d = {"n": 0}

        def proj_head(h, wsrc, buf, hoff, sem_w, dst, ci, l2, par):
            sp = snap[par]
            slot = sem_w[2:]
            b.wait("sync", "s_tensor", max(sp["t"], slot_last[slot]))
            w = b.op("sync", lambda e, h=h, buf=buf, hoff=hoff, wsrc=wsrc: e.dma_start(
                out=buf[:, hoff:hoff + D], in_=wsrc[h]), inc=sem_w, n=16)
            pp = PS[par]
            b.wait("tensor", sem_w, w)
            b.guard(f"ps{par}", "tensor")
            for dc in range(NDC):
                nt = b.op("tensor", lambda e, dc=dc, buf=buf, hoff=hoff, pp=pp: e.matmul(
                    pp[:, :], buf[:, hoff + dc * P:hoff + (dc + 1) * P],
                    xs[:, dc * TT:(dc + 1) * TT],
                    start=(dc == 0), stop=(dc == NDC - 1)),
                    inc="s_tensor" if dc == NDC - 1 else None)
            b.touch(f"ps{par}", "tensor", nt)
            mm_done = nt
            slot_last[slot] = mm_done
            b.wait("vector", "s_gpsimd", sp["g"])
            b.wait("vector", "s_scalar", sp["s"])
            cl = silb[:, 0:TT + 8]
            nv = b.op("vector", lambda e, ci=ci, h=h, cl=cl: e.tensor_copy(
                out=cl[:, 0:3],
                in_=convt[:, (ci * NH + h) * 3:(ci * NH + h) * 3 + 3]),
                inc="s_vector")
            b.wait("vector", "s_tensor", mm_done)
            nv = b.op("vector", lambda e, ci=ci, h=h, pp=pp: e.tensor_copy(
                out=convt[:, (ci * NH + h) * 3:(ci * NH + h) * 3 + 3],
                in_=pp[:, TT - 3:TT]), inc="s_vector")
            nv = b.op("vector", lambda e, pp=pp, cl=cl: e.tensor_copy(
                out=cl[:, 3:3 + TT], in_=pp[:, :]), inc="s_vector")
            b.touch(f"ps{par}", "vector", nv)
            b.wait("vector", "s_scalar", conv_read["n"])
            cw0 = (ci * NDC + h) * 4
            cacc = silb[:, 2 * TT:2 * TT + TT]
            nv = b.op("vector", lambda e, cw0=cw0, cl=cl, cacc=cacc: e.tensor_scalar(
                out=cacc, in0=cl[:, 0:TT],
                scalar1=convw[:, cw0:cw0 + 1], scalar2=None, op0=ALU.mult),
                inc="s_vector")
            for i in range(1, 4):
                nv = b.op("vector", lambda e, cw0=cw0, i=i, cl=cl, cacc=cacc: e.scalar_tensor_tensor(
                    out=cacc, in0=cl[:, i:i + TT],
                    scalar=convw[:, cw0 + i:cw0 + i + 1],
                    in1=cacc, op0=ALU.mult, op1=ALU.add), inc="s_vector")
            b.wait("scalar", "s_vector", nv)
            b.wait("scalar", "s_tensor", sp["t"])
            na = b.op("scalar", lambda e, dst=dst, cacc=cacc: e.activation(
                out=dst, in_=cacc, func=AF.Silu),
                inc="s_scalar")
            conv_read["n"] = na
            return na

        def l2_pair(dst0, dst1, s0, s1):
            sqb = lambda par: wC[:, 1024 + par * TT:1024 + (par + 1) * TT]
            b.wait("vector", "s_scalar", max(s0, s1))
            b.wait("vector", "s_tensor", max(l2_last[0], l2_last[1]))
            for par, dst in ((0, dst0), (1, dst1)):
                ng = b.op("vector", lambda e, dst=dst, par=par: e.tensor_mul(
                    out=sqb(par), in0=dst, in1=dst), inc="s_vector")
            b.wait("tensor", "s_vector", ng)
            for par in (0, 1):
                b.guard(f"ps{par}", "tensor")
                nt = b.op("tensor", lambda e, par=par: e.matmul(
                    PS[par][0:1, 0:TT], ones[:], sqb(par),
                    start=True, stop=True), inc="s_tensor")
                l2_last[par] = nt
                b.touch(f"ps{par}", "tensor", nt)
            b.wait("scalar", "s_tensor", nt)
            for par in (0, 1):
                na = b.op("scalar", lambda e, par=par: e.activation(
                    out=rows[0:1, RW_L2P[par]:RW_L2P[par] + TT],
                    in_=PS[par][0:1, 0:TT],
                    func=AF.Ln, bias=cbias[0:1, 1:2]), inc="s_scalar")
                b.touch(f"ps{par}", "scalar", na)
            for par in (0, 1):
                na = b.op("scalar", lambda e, par=par: e.activation(
                    out=rowsb[0:1, RB_L2P[par]:RB_L2P[par] + TT],
                    in_=rows[0:1, RW_L2P[par]:RW_L2P[par] + TT], func=AF.Exp,
                    scale=-0.5), inc="s_scalar")
            b.wait("tensor", "s_scalar", na)
            res = []
            for par, dst in ((0, dst0), (1, dst1)):
                b.guard(f"ps{par}", "tensor")
                nt = b.op("tensor", lambda e, par=par: e.matmul(
                    PS[par][:, :], onesr[:],
                    rowsb[0:1, RB_L2P[par]:RB_L2P[par] + TT],
                    start=True, stop=True), inc="s_tensor")
                b.touch(f"ps{par}", "tensor", nt)
                b.wait("vector", "s_tensor", nt)
                nv = b.op("vector", lambda e, dst=dst, par=par: e.tensor_mul(
                    out=dst, in0=dst, in1=PS[par][:, :]), inc="s_vector")
                b.touch(f"ps{par}", "vector", nv)
                res.append(nv)
            return res[0], res[1]

        def gate_silu(h, par):
            sp = snap[par]
            b.wait("sync", "s_tensor", max(sp["t"], slot_last["wb1"]))
            w = b.op("sync", lambda e, h=h: e.dma_start(
                out=wB[:, D:2 * D], in_=wgb[h]), inc="s_wb1", n=16)
            pp = PS[par]
            b.wait("tensor", "s_wb1", w)
            b.guard(f"ps{par}", "tensor")
            for dc in range(NDC):
                nt = b.op("tensor", lambda e, dc=dc, pp=pp: e.matmul(
                    pp[:, :], wB[:, D + dc * P:D + (dc + 1) * P],
                    xs[:, dc * TT:(dc + 1) * TT],
                    start=(dc == 0), stop=(dc == NDC - 1)),
                    inc="s_tensor" if dc == NDC - 1 else None)
            b.touch(f"ps{par}", "tensor", nt)
            slot_last["wb1"] = nt
            b.wait("scalar", "s_tensor", nt)
            b.wait("scalar", "s_gpsimd", sp["g"])
            na = b.op("scalar", lambda e, h=h, pp=pp, par=par: e.activation(
                out=hsl(HC_GSIL + par * TT, TT), in_=pp[:, :], func=AF.Silu,
                bias=bgc[:, h:h + 1]), inc="s_scalar")
            b.touch(f"ps{par}", "scalar", na)
            return na

        def g_decay(h, par):
            fsg = slice(FS_G + par * TT, FS_G + (par + 1) * TT)
            pp = PS[par]
            b.guard(f"ps{par}", "tensor")
            b.wait("tensor", "s_vector", t1f_done_d["n"])
            nt = b.op("tensor", lambda e, h=h, pp=pp: e.matmul(
                pp[:, :], wf2b[:, h * P:(h + 1) * P], t1f[:],
                start=True, stop=True), inc="s_tensor")
            b.touch(f"ps{par}", "tensor", nt)
            b.wait("scalar", "s_tensor", nt)
            na = b.op("scalar", lambda e, h=h, pp=pp, fsg=fsg: e.activation(
                out=fscr[:, fsg], in_=pp[:, :], func=AF.Exp,
                bias=dtbc[:, h:h + 1]), inc="s_scalar")
            b.touch(f"ps{par}", "scalar", na)
            na = b.op("scalar", lambda e, fsg=fsg: e.activation(
                out=fscr[:, fsg], in_=fscr[:, fsg],
                func=AF.Ln, bias=cbias[:, 2:3]), inc="s_scalar")
            b.wait("vector", "s_scalar", na)
            nv = b.op("vector", lambda e, h=h, par=par, fsg=fsg: e.tensor_scalar(
                out=hsl(HC_GH + par * TT, TT), in0=fscr[:, fsg],
                scalar1=eac[:, h:h + 1], scalar2=None, op0=ALU.mult),
                inc="s_vector")
            return nv

        def ladder_gen(h, par, g_done, qk_done, lc):
            sp = snap[par]
            cg = cgw[:, par * TT:(par + 1) * TT]
            cgl = cglw[:, par * TT:(par + 1) * TT]
            ex = hsl(HC_EXW + par * TT, TT)
            khp = hsl(HC_KH + par * TT, TT)
            qhp = hsl(HC_QH + par * TT, TT)

            exbufs = [ex, wC[:, par * TT:(par + 1) * TT]]
            exi = {"i": 0}
            last_act = {"n": 0}
            last_mul = {0: 0, 1: 0}

            def exp_to_ex(src, scale=1.0, bias=None):
                exi["i"] ^= 1
                exc = exbufs[exi["i"]]
                b.wait("scalar", "s_vector", last_mul[exi["i"]])
                bap = 0.0 if bias is None else bias
                na = b.op("scalar", lambda e, src=src, scale=scale, bap=bap, exc=exc: e.activation(
                    out=exc, in_=src, func=AF.Exp, scale=scale, bias=bap),
                    inc="s_scalar")
                last_act["n"] = na
                return na

            def mults(tiles):
                exc = exbufs[exi["i"]]
                b.wait("vector", "s_scalar", last_act["n"])
                nv = 0
                for tid, srck in tiles:
                    src = khp if srck == "k" else qhp
                    nv = b.op("vector", lambda e, tid=tid, src=src, exc=exc: e.tensor_mul(
                        out=lad(tid, par), in0=src, in1=exc), inc="s_vector")
                last_mul[exi["i"]] = nv
                return nv

            b.wait("vector", "s_vector", g_done)
            b.wait("vector", "s_tensor", sp["t"])
            nv = b.op("vector", lambda e, cg=cg, par=par: e.tensor_tensor_scan(
                out=cg, data0=hsl(HC_GH + par * TT, TT),
                data1=hsl(HC_GH + par * TT, TT), initial=0.0,
                op0=ALU.add, op1=ALU.bypass), inc="s_vector")
            for cc in range(NCHUNK - 1, 0, -1):
                nv = b.op("vector", lambda e, cc=cc, cg=cg: e.tensor_sub(
                    out=cg[:, cc * CH:(cc + 1) * CH],
                    in0=cg[:, cc * CH:(cc + 1) * CH],
                    in1=cg[:, cc * CH - 1:cc * CH].broadcast_to([P, CH])),
                    inc="s_vector")
            cg_done = nv

            def rebase(m):
                nb = TT // m
                d3 = cgl.rearrange("p (b i) -> p b i", i=m)
                c3 = cg.rearrange("p (b i) -> p b i", i=m)
                b.wait("vector", "s_scalar", last_act["n"])
                n1 = b.op("vector", lambda e, d3=d3, c3=c3, nb=nb, m=m: e.tensor_copy(
                    out=d3[:, 0:nb:CH // m, :], in_=c3[:, 0:nb:CH // m, :]),
                    inc="s_vector")
                for c in range(NCHUNK):
                    base = c * (CH // m)
                    n1 = b.op("vector", lambda e, d3=d3, c3=c3, m=m, base=base, cg=cg, c_=c: e.tensor_sub(
                        out=d3[:, base + 1:base + CH // m, :],
                        in0=c3[:, base + 1:base + CH // m, :],
                        in1=cg[:, c_ * CH + m - 1:(c_ + 1) * CH - 1:m]
                            .unsqueeze(2)
                            .broadcast_to([P, CH // m - 1, m])), inc="s_vector")
                return n1

            def endsub(m):
                b.wait("vector", "s_scalar", last_act["n"])
                n1 = b.op("vector", lambda e, m=m, cgl=cgl, cg=cg: e.tensor_sub(
                    out=cgl.rearrange("p (b i) -> p b i", i=m),
                    in0=cg[:, m - 1::m].unsqueeze(2)
                        .broadcast_to([P, TT // m, m]),
                    in1=cg.rearrange("p (b i) -> p b i", i=m)), inc="s_vector")
                return n1

            b.wait("scalar", "s_vector", cg_done)
            exp_to_ex(hsl(HC_GH + par * TT, TT))
            lc.append(mults([(L_KP1, "k"), (L_QP1, "q")]))
            yield
            nv = rebase(2)
            b.wait("scalar", "s_vector", nv)
            exp_to_ex(cgl)
            mults([(L_KP2, "k"), (L_QP2, "q")])
            nv = endsub(2)
            b.wait("scalar", "s_vector", nv)
            exp_to_ex(cgl)
            lc.append(mults([(L_KM2, "k")]))
            yield
            nv = rebase(4)
            b.wait("scalar", "s_vector", nv)
            exp_to_ex(cgl)
            lc.append(mults([(L_KP4, "k"), (L_QP4, "q")]))
            yield
            nv = endsub(4)
            b.wait("scalar", "s_vector", nv)
            exp_to_ex(cgl)
            lc.append(mults([(L_KM4, "k")]))
            yield
            nv = rebase(8)
            b.wait("scalar", "s_vector", nv)
            exp_to_ex(cgl)
            lc.append(mults([(L_KP8, "k"), (L_QP8, "q")]))
            yield
            nv = endsub(8)
            b.wait("scalar", "s_vector", nv)
            exp_to_ex(cgl)
            lc.append(mults([(L_KM8, "k")]))
            yield
            nv = endsub(16)
            b.wait("scalar", "s_vector", nv)
            exp_to_ex(cgl)
            lc.append(mults([(L_KM16, "k")]))
            yield
            exp_to_ex(cg)
            lc.append(mults([(L_KAP, "k"), (L_QTL, "q")]))
            yield
            nv = endsub(CH)
            b.wait("scalar", "s_vector", nv)
            exp_to_ex(cgl)
            ng = mults([(L_KBR, "k")])
            lc.append(ng)
            return ng

        def chunk_gen(h, c, par, lc, qk_done, v_done, gs_done, prev_sdone):
            t0 = par * TT + c * CH
            chs = slice(t0, t0 + CH)
            pA = PS[2 + par]
            pB = PS[4 + par]
            pam = f"ps{2 + par}"
            pbm = f"ps{4 + par}"
            acc = accf[:, (2 * par) * CH:(2 * par + 1) * CH]
            accq = accf[:, (2 * par + 1) * CH:(2 * par + 2) * CH]
            Scur = Sst[:, h * CH:(h + 1) * CH]
            Sb = Ssb[:, par * CH:(par + 1) * CH]
            betacol = bett[:, c * NH + h:c * NH + h + 1]
            fch = fscr[:, FS_CH[par]:FS_CH[par] + CH]
            flam = fscr[:, FS_CH[par] + CH:FS_CH[par] + CH + 1]

            T_ = csl(par, 0); TTr = csl(par, 1); MqT = csl(par, 2)
            W_ = csl(par, 3)
            Pa = csl(par, 4); Ua = csl(par, 5)
            Pb = csl(par, 6); Ub = csl(par, 7)
            Vt = csl(par, 8); KbT = csl(par, 9); rfb = csl(par, 10)

            b.wait("gpsimd", "s_vector", prev_sdone)
            ng = b.op("gpsimd", lambda e, Sb=Sb, Scur=Scur: e.tensor_copy(
                out=Sb, in_=Scur), inc="s_gpsimd")
            sb_done = ng

            qsl = hsl(HC_QH + par * TT, TT)[:, c * CH:(c + 1) * CH]
            ksl = hsl(HC_KH + par * TT, TT)[:, c * CH:(c + 1) * CH]

            def lsl(i):
                return lad(i, par)[:, c * CH:(c + 1) * CH]
            lvT = [(lsl(L_KP1), ksl, CST_MX1),
                   (lsl(L_KP2), lsl(L_KM2), CST_MX2),
                   (lsl(L_KP4), lsl(L_KM4), CST_MX4),
                   (lsl(L_KP8), lsl(L_KM8), CST_MX8),
                   (lsl(L_KP8), lsl(L_KM16), CST_MX8S)]
            lvQ = [(qsl, ksl, CST_IDENT),
                   (lsl(L_QP1), ksl, CST_TX1),
                   (lsl(L_QP2), lsl(L_KM2), CST_TX2),
                   (lsl(L_QP4), lsl(L_KM4), CST_TX4),
                   (lsl(L_QP8), lsl(L_KM8), CST_TX8),
                   (lsl(L_QP8), lsl(L_KM16), CST_TX8S)]
            b.wait("tensor", "s_vector", qk_done)
            pflip = 0
            NEED_T = [0, 1, 3, 5, 6]
            NEED_Q = [None, 0, 1, 3, 5, 6]

            def levels(lv, accd, outd, transposed, needs):
                nonlocal pflip
                for li, (tside, sside, mi) in enumerate(lv):
                    need = needs[li]
                    if need is not None:
                        while len(lc) <= need:
                            yield
                        b.wait("tensor", "s_vector", lc[need])
                    pl = pA if pflip % 2 == 0 else pB
                    plm = pam if pflip % 2 == 0 else pbm
                    pflip += 1
                    b.guard(plm, "tensor")
                    if transposed:
                        nt = b.op("tensor", lambda e, pl=pl, tside=tside, sside=sside: e.matmul(
                            pl[:, 0:CH], sside, tside, start=True, stop=True),
                            inc="s_tensor")
                    else:
                        nt = b.op("tensor", lambda e, pl=pl, tside=tside, sside=sside: e.matmul(
                            pl[:, 0:CH], tside, sside, start=True, stop=True),
                            inc="s_tensor")
                    b.touch(plm, "tensor", nt)
                    b.wait("vector", "s_tensor", nt)
                    if li == 0:
                        nv = b.op("vector", lambda e, pl=pl, mi=mi, accd=accd: e.tensor_mul(
                            out=accd, in0=pl[:, 0:CH], in1=mask(mi)),
                            inc="s_vector")
                    else:
                        nv = b.op("vector", lambda e, pl=pl, mi=mi, fch=fch: e.tensor_mul(
                            out=fch, in0=pl[:, 0:CH], in1=mask(mi)), inc="s_vector")
                        dd = accd if li < len(lv) - 1 else outd
                        nv = b.op("vector", lambda e, accd=accd, fch=fch, dd=dd: e.tensor_add(
                            out=dd, in0=accd, in1=fch), inc="s_vector")
                    b.touch(plm, "vector", nv)
                    yield

            yield from levels(lvT, acc, acc, False, NEED_T)
            nv = b.op("vector", lambda e, T_=T_, acc=acc, betacol=betacol: e.tensor_scalar(
                out=T_, in0=acc, scalar1=betacol, scalar2=None, op0=ALU.mult),
                inc="s_vector")
            t_done = nv
            yield from levels(lvQ, accq, MqT, True, NEED_Q)

            b.wait("tensor", "s_vector", t_done)
            b.guard(f"ps7_{par}", "tensor")
            nt = b.op("tensor", lambda e, T_=T_, par=par: e.transpose(
                PS7P[par][:, 0:CH], T_, ident), inc="s_tensor")
            b.touch(f"ps7_{par}", "tensor", nt)
            b.wait("scalar", "s_tensor", nt)
            na = b.op("scalar", lambda e, TTr=TTr, par=par: e.copy(
                out=TTr, in_=PS7P[par][:, 0:CH]), inc="s_scalar")
            b.touch(f"ps7_{par}", "scalar", na)
            tt_done = na
            yield

            b.wait("tensor", "s_scalar", v_done)
            b.guard(f"ps7_{par}", "tensor")
            nt = b.op("tensor", lambda e, par=par, chs=chs: e.transpose(
                PS7P[par][:, CH:2 * CH],
                hsl(HC_VH, 2 * TT)[:, chs], ident), inc="s_tensor")
            b.touch(f"ps7_{par}", "tensor", nt)
            b.wait("scalar", "s_tensor", nt)
            na = b.op("scalar", lambda e, Vt=Vt, par=par: e.copy(
                out=Vt, in_=PS7P[par][:, CH:2 * CH]),
                inc="s_scalar")
            b.touch(f"ps7_{par}", "scalar", na)
            vt_done = na
            yield

            while len(lc) <= 7:
                yield
            b.wait("tensor", "s_vector", lc[7])
            b.wait("tensor", "s_gpsimd", sb_done)
            b.guard(pam, "tensor")
            nt = b.op("tensor", lambda e, Sb=Sb, par=par, c=c: e.matmul(
                pA[:, 2 * CH:3 * CH],
                lad(L_KAP, par)[:, c * CH:(c + 1) * CH], Sb,
                start=True, stop=True), inc="s_tensor")
            b.touch(pam, "tensor", nt)
            ks_done = nt
            b.wait("vector", "s_scalar", vt_done)
            b.wait("vector", "s_tensor", ks_done)
            nv = b.op("vector", lambda e, Vt=Vt, par=par, fch=fch, pA=pA: e.scalar_tensor_tensor(
                out=fch, in0=pA[:, 2 * CH:3 * CH], scalar=-1.0,
                in1=Vt, op0=ALU.mult, op1=ALU.add), inc="s_vector")
            b.touch(pam, "vector", nv)
            nv = b.op("vector", lambda e, W_=W_, betacol=betacol, fch=fch: e.tensor_scalar(
                out=W_, in0=fch, scalar1=betacol, scalar2=None, op0=ALU.mult),
                inc="s_vector")
            w_rhs = nv
            yield

            b.wait("tensor", "s_vector", w_rhs)
            b.wait("tensor", "s_scalar", tt_done)
            cur_p, cur_u = T_, TTr
            for stg in range(4):
                b.guard(pbm, "tensor")
                nt = b.op("tensor", lambda e, cur_u=cur_u, W_=W_, pB=pB: e.matmul(
                    pB[:, 0:CH], cur_u, W_, start=True, stop=True),
                    inc="s_tensor")
                b.touch(pbm, "tensor", nt)
                b.wait("vector", "s_tensor", nt)
                sgn = -1.0 if stg == 0 else 1.0
                nv = b.op("vector", lambda e, W_=W_, pB=pB, sgn=sgn: e.scalar_tensor_tensor(
                    out=W_, in0=pB[:, 0:CH], scalar=sgn, in1=W_,
                    op0=ALU.mult, op1=ALU.add), inc="s_vector")
                b.touch(pbm, "vector", nv)
                w_upd = nv
                if stg < 3:
                    b.guard(pam, "tensor")
                    nt = b.op("tensor", lambda e, cur_u=cur_u, cur_p=cur_p, pA=pA: e.matmul(
                        pA[:, 0:CH], cur_u, cur_p, start=True, stop=True),
                        inc="s_tensor")
                    nt = b.op("tensor", lambda e, cur_u=cur_u, cur_p=cur_p, pA=pA: e.matmul(
                        pA[:, CH:2 * CH], cur_p, cur_u, start=True, stop=True),
                        inc="s_tensor")
                    b.touch(pam, "tensor", nt)
                    b.wait("scalar", "s_tensor", nt)
                    dst_p = Pa if cur_p in (T_, Pb) else Pb
                    dst_u = Ua if cur_u in (TTr, Ub) else Ub
                    na = b.op("scalar", lambda e, pA=pA, dst_p=dst_p: e.copy(
                        out=dst_p, in_=pA[:, 0:CH]), inc="s_scalar")
                    na = b.op("scalar", lambda e, pA=pA, dst_u=dst_u: e.copy(
                        out=dst_u, in_=pA[:, CH:2 * CH]), inc="s_scalar")
                    b.touch(pam, "scalar", na)
                    cur_p, cur_u = dst_p, dst_u
                    b.wait("tensor", "s_scalar", na)
                b.wait("tensor", "s_vector", w_upd)
                yield

            b.wait("tensor", "s_vector", lc[7])
            b.guard(pbm, "tensor")
            nt = b.op("tensor", lambda e, Sb=Sb, par=par, c=c, pB=pB: e.matmul(
                pB[:, CH:2 * CH], Sb, lad(L_QTL, par)[:, c * CH:(c + 1) * CH],
                start=True, stop=False), inc="s_tensor")
            nt = b.op("tensor", lambda e, W_=W_, MqT=MqT, pB=pB: e.matmul(
                pB[:, CH:2 * CH], W_, MqT, start=False, stop=True),
                inc="s_tensor")
            b.touch(pbm, "tensor", nt)
            o_done = nt
            b.wait("scalar", "s_tensor", o_done)
            if dbg == "c0" and h == 0 and c == 0:
                na = b.op("scalar", lambda e, pB=pB: e.copy(
                    out=csl(par, 11), in_=pB[:, CH:2 * CH]), inc="s_scalar")
                b.touch(pbm, "scalar", na)
            while len(lc) <= 8:
                yield
            b.wait("scalar", "s_vector", lc[8])
            na = b.op("scalar", lambda e, pB=pB, par=par: e.activation(
                out=hsl(HC_EXW + par * TT, CH), in_=pB[:, CH:2 * CH],
                func=AF.Square), inc="s_scalar")
            b.touch(pbm, "scalar", na)
            b.wait("tensor", "s_scalar", na)
            b.guard(pbm, "tensor")
            nt = b.op("tensor", lambda e, pB=pB, par=par: e.matmul(
                pB[0:1, 2 * CH:3 * CH], ones[:], hsl(HC_EXW + par * TT, CH),
                start=True, stop=True), inc="s_tensor")
            b.touch(pbm, "tensor", nt)
            b.wait("scalar", "s_tensor", nt)
            na = b.op("scalar", lambda e, pB=pB, par=par: e.activation(
                out=rows[0:1, RW_RMS[par]:RW_RMS[par] + CH],
                in_=pB[0:1, 2 * CH:3 * CH], func=AF.Ln,
                scale=SCALE * SCALE / 128.0, bias=cbias[0:1, 0:1]), inc="s_scalar")
            b.touch(pbm, "scalar", na)
            na = b.op("scalar", lambda e, par=par: e.activation(
                out=rowsb[0:1, RB_RMS[par]:RB_RMS[par] + CH],
                in_=rows[0:1, RW_RMS[par]:RW_RMS[par] + CH], func=AF.Exp,
                scale=-0.5), inc="s_scalar")
            b.wait("tensor", "s_scalar", na)
            b.guard(pbm, "tensor")
            nt = b.op("tensor", lambda e, pB=pB, par=par: e.matmul(
                pB[:, 3 * CH:4 * CH], scaler[:],
                rowsb[0:1, RB_RMS[par]:RB_RMS[par] + CH], start=True,
                stop=True), inc="s_tensor")
            b.touch(pbm, "tensor", nt)
            b.wait("vector", "s_tensor", nt)
            nv = b.op("vector", lambda e, pB=pB, rfb=rfb: e.tensor_copy(
                out=rfb, in_=pB[:, 3 * CH:4 * CH]), inc="s_vector")
            b.touch(pbm, "vector", nv)
            nv = b.op("vector", lambda e, pB=pB, rfb=rfb, fch=fch: e.scalar_tensor_tensor(
                out=fch, in0=pB[:, CH:2 * CH], scalar=onwc[:, 0:1], in1=rfb,
                op0=ALU.mult, op1=ALU.mult), inc="s_vector")
            b.touch(pbm, "vector", nv)
            b.wait("gpsimd", "s_vector", nv)
            b.wait("gpsimd", "s_scalar", gs_done)
            ng = b.op("gpsimd", lambda e, h=h, c=c, par=par, fch=fch: e.tensor_mul(
                out=hsl(HC_ONB + h * TT + c * CH, CH), in0=fch,
                in1=hsl(HC_GSIL + par * TT + c * CH, CH)), inc="s_gpsimd")
            on_done = ng
            yield

            b.wait("tensor", "s_vector", lc[8])
            b.guard(f"ps7_{par}", "tensor")
            nt = b.op("tensor", lambda e, par=par, c=c: e.transpose(
                PS7P[par][:, 2 * CH:3 * CH],
                lad(L_KBR, par)[:, c * CH:(c + 1) * CH], ident),
                inc="s_tensor")
            b.touch(f"ps7_{par}", "tensor", nt)
            b.wait("scalar", "s_tensor", nt)
            na = b.op("scalar", lambda e, KbT=KbT, par=par: e.copy(
                out=KbT, in_=PS7P[par][:, 2 * CH:3 * CH]),
                inc="s_scalar")
            b.touch(f"ps7_{par}", "scalar", na)
            b.wait("tensor", "s_scalar", na)
            b.guard(pam, "tensor")
            nt = b.op("tensor", lambda e, KbT=KbT, W_=W_, par=par, pA=pA: e.matmul(
                pA[:, 3 * CH:4 * CH], KbT, W_,
                start=True, stop=True), inc="s_tensor")
            b.touch(pam, "tensor", nt)
            na = b.op("scalar", lambda e, par=par, c=c, flam=flam: e.activation(
                out=flam, in_=cgw[:, par * TT + (c + 1) * CH - 1:
                                  par * TT + (c + 1) * CH], func=AF.Exp),
                inc="s_scalar")
            yield
            b.wait("vector", "s_scalar", na)
            b.wait("vector", "s_tensor", nt)
            nv = b.op("vector", lambda e, Scur=Scur, par=par, flam=flam, pA=pA: e.scalar_tensor_tensor(
                out=Scur, in0=Scur, scalar=flam,
                in1=pA[:, 3 * CH:4 * CH],
                op0=ALU.mult, op1=ALU.add), inc="s_vector")
            b.touch(pam, "vector", nv)
            s_done = nv
            if dbg == "c0" and h == 0 and c == 0:
                for eng in ("s_vector", "s_gpsimd", "s_scalar", "s_tensor"):
                    b.wait("gpsimd", eng, b.counts[eng])
                dumps = [T_, TTr, MqT, W_, Vt, KbT,
                         lad(L_KM16, par)[:, 0:CH],
                         lad(L_KP8, par)[:, 0:CH], Sb, rfb]
                for di, srcd in enumerate(dumps):
                    b.op("gpsimd", lambda e, di=di, srcd=srcd: e.dma_start(
                        out=outT[di * P:(di + 1) * P, 0:CH], in_=srcd),
                        inc="s_st0", n=16)
                b.op("gpsimd", lambda e: e.dma_start(
                    out=outT[10 * P:11 * P, 0:TT], in_=cgw[:, 0:TT]),
                    inc="s_st0", n=16)
                b.op("gpsimd", lambda e: e.dma_start(
                    out=outT[11 * P:12 * P, 0:2 * CH], in_=accf[:, 0:2 * CH]),
                    inc="s_st0", n=16)
                b.op("gpsimd", lambda e: e.dma_start(
                    out=outT[12 * P:13 * P, 0:CH],
                    in_=hsl(HC_ONB, CH)), inc="s_st0", n=16)
                b.op("gpsimd", lambda e: e.dma_start(
                    out=outT[14 * P:15 * P, 0:CH],
                    in_=csl(par, 11)), inc="s_st0", n=16)
                w = b.op("gpsimd", lambda e: e.dma_start(
                    out=outT[13 * P:14 * P, 0:CH],
                    in_=hsl(HC_GSIL, CH)), inc="s_st0", n=16)
                b.wait("gpsimd", "s_st0", w)
            return on_done, s_done

        def dbg_store(it, src, dtype_bf16):
            b.wait("gpsimd", "s_vector", b.counts["s_vector"])
            b.wait("gpsimd", "s_gpsimd", b.counts["s_gpsimd"])
            b.wait("gpsimd", "s_scalar", b.counts["s_scalar"])
            b.wait("gpsimd", "s_tensor", b.counts["s_tensor"])
            w = 0
            for dc in range(NDC):
                w = b.op("gpsimd", lambda e, dc=dc, it=it, src=src: e.dma_start(
                    out=outT[dc * P:(dc + 1) * P, it * TT:(it + 1) * TT],
                    in_=src[:, dc * TT:(dc + 1) * TT]), inc="s_st0", n=16)
            return w

        def drive(*gens):
            gens = list(gens)
            res = [None] * len(gens)
            done = [False] * len(gens)
            while not all(done):
                for i, g in enumerate(gens):
                    if done[i]:
                        continue
                    try:
                        next(g)
                    except StopIteration as e:
                        done[i] = True
                        res[i] = e.value
            return res

        last_out_ev = prep_vec
        last_ln_gp = 0
        for it in range(ntile):
            b.wait("sync", "s_vector", last_out_ev)
            if it == 0:
                b.wait("sync", "s_dmaw", prep_dma)
                b.wait("sync", "s_stw", prep_stw)
                b.wait("gpsimd", "s_dmaw", prep_dma)
                b.wait("vector", "s_stw", prep_stw)
            nx = b.op("sync", lambda e, it=it: e.dma_start(
                out=xres[:].rearrange("p (a t) -> p a t", a=NDC),
                in_=xT.rearrange("(a p) T -> p a T", p=P)[:, :, it * TT:(it + 1) * TT]),
                inc="s_dmax", n=16)
            b.wait("gpsimd", "s_dmax", nx)
            b.wait("vector", "s_dmax", nx)
            xs_done, xs_v = ln_pass("s_dmax", nx)
            if dbg == "ln1":
                b.wait("gpsimd", "s_gpsimd", xs_done)
                dbg_store(it, xs, True)
                continue
            ffn1_done = ffn_pass(it, first=True, xs_v=xs_v)
            if dbg == "ffn1":
                b.wait("gpsimd", "s_vector", ffn1_done)
                dbg_store(it, xres, False)
                continue

            b.wait("gpsimd", "s_vector", ffn1_done)
            xs_done, xs_v = ln_pass("s_vector", ffn1_done)

            b.guard("ps0", "tensor")
            for dc in range(NDC):
                b.wait("tensor", "s_vector", xs_v[dc])
                nt = b.op("tensor", lambda e, dc=dc: e.matmul(
                    PS[0][0:NH, 0:TT], wbb[:, dc * NH:(dc + 1) * NH],
                    xs[:, dc * TT:(dc + 1) * TT],
                    start=(dc == 0), stop=(dc == NDC - 1)),
                    inc="s_tensor" if dc == NDC - 1 else None)
            b.touch("ps0", "tensor", nt)
            b.wait("scalar", "s_tensor", nt)
            na = b.op("scalar", lambda e: e.activation(
                out=betf[:, :], in_=PS[0][0:NH, 0:TT], func=AF.Sigmoid),
                inc="s_scalar")
            b.touch("ps0", "scalar", na)
            bet_done = na
            for c in range(NCHUNK):
                b.wait("tensor", "s_scalar", bet_done if c == 0 else 0)
                b.guard("ps7_0", "tensor")
                nt = b.op("tensor", lambda e, c=c: e.transpose(
                    PS7P[0][:, 3 * CH:3 * CH + NH],
                    betf[0:NH, c * CH:(c + 1) * CH], ident[0:NH, 0:NH]), inc="s_tensor")
                b.touch("ps7_0", "tensor", nt)
                b.wait("vector", "s_tensor", nt)
                nv = b.op("vector", lambda e, c=c: e.tensor_copy(
                    out=bett[:, c * NH:(c + 1) * NH],
                    in_=PS7P[0][:, 3 * CH:3 * CH + NH]), inc="s_vector")
                b.touch("ps7_0", "vector", nv)
            b.guard("ps1", "tensor")
            for dc in range(NDC):
                nt = b.op("tensor", lambda e, dc=dc: e.matmul(
                    PS[1][:, 0:TT], wf1b[:, dc * P:(dc + 1) * P],
                    xs[:, dc * TT:(dc + 1) * TT],
                    start=(dc == 0), stop=(dc == NDC - 1)),
                    inc="s_tensor" if dc == NDC - 1 else None)
            b.touch("ps1", "tensor", nt)
            b.wait("vector", "s_tensor", nt)
            nv = b.op("vector", lambda e: e.tensor_copy(
                out=t1f[:], in_=PS[1][:, 0:TT]), inc="s_vector")
            b.touch("ps1", "vector", nv)
            t1f_done_d["n"] = nv

            snap[0] = dict(t=b.counts["s_tensor"], v=b.counts["s_vector"],
                           s=b.counts["s_scalar"], g=b.counts["s_gpsimd"])
            snap[1] = dict(snap[0])

            prev_s = {0: 0, 1: 0}
            for hp in range(NH // 2):
                h0, h1 = 2 * hp, 2 * hp + 1
                q0 = proj_head(h0, wqb, wA, 0, "s_wa0",
                               hsl(HC_QH, TT), 0, False, 0)
                q1 = proj_head(h1, wqb, wC, 2048, "s_wcq",
                               hsl(HC_QH + TT, TT), 0, False, 1)
                k0 = proj_head(h0, wkb, wA, D, "s_wa1",
                               hsl(HC_KH, TT), 1, False, 0)
                k1 = proj_head(h1, wkb, wC, 4096, "s_wck",
                               hsl(HC_KH + TT, TT), 1, False, 1)
                v0 = proj_head(h0, wvb, wB, 0, "s_wb0",
                               hsl(HC_VH, TT), 2, False, 0)
                v1 = proj_head(h1, wvb, wC, 6144, "s_wcv",
                               hsl(HC_VH + TT, TT), 2, False, 1)
                gs0 = gate_silu(h0, 0)
                gs1 = gate_silu(h1, 1)
                q0, q1 = l2_pair(hsl(HC_QH, TT), hsl(HC_QH + TT, TT), q0, q1)
                k0, k1 = l2_pair(hsl(HC_KH, TT), hsl(HC_KH + TT, TT), k0, k1)
                g0 = g_decay(h0, 0)
                g1 = g_decay(h1, 1)
                lc0, lc1 = [], []
                sd0, sd1 = prev_s[0], prev_s[1]
                res = drive(ladder_gen(h0, 0, g0, max(q0, k0), lc0),
                            ladder_gen(h1, 1, g1, max(q1, k1), lc1),
                            chunk_gen(h0, 0, 0, lc0, max(q0, k0), v0, gs0, sd0),
                            chunk_gen(h1, 0, 1, lc1, max(q1, k1), v1, gs1, sd1))
                (o0, sd0), (o1, sd1) = res[2], res[3]
                for c in range(1, NCHUNK):
                    (o0, sd0), (o1, sd1) = drive(
                        chunk_gen(h0, c, 0, lc0, 0, v0, gs0, sd0),
                        chunk_gen(h1, c, 1, lc1, 0, v1, gs1, sd1))
                prev_s[0], prev_s[1] = sd0, sd1
                snap[0] = dict(t=b.counts["s_tensor"], v=b.counts["s_vector"],
                               s=b.counts["s_scalar"], g=b.counts["s_gpsimd"])
                snap[1] = dict(snap[0])
                if dbg in ("h0", "c0", "hall2") and hp == 0:
                    break
                if dbg == "hall4" and hp == 1:
                    break

            if dbg == "c0":
                continue
            if dbg in ("hall", "hall4", "hall2"):
                for eng in ("s_vector", "s_gpsimd", "s_scalar", "s_tensor"):
                    b.wait("gpsimd", eng, b.counts[eng])
                for hh in range(NH):
                    b.op("gpsimd", lambda e, hh=hh, it=it: e.dma_start(
                        out=outT[hh * P:(hh + 1) * P, it * TT:(it + 1) * TT],
                        in_=hsl(HC_ONB + hh * TT, TT)), inc="s_st0", n=16)
                if dbg == "hall2":
                    b.op("gpsimd", lambda e, it=it: e.dma_start(
                        out=outT[2 * P:3 * P, it * TT:(it + 1) * TT],
                        in_=silb[:, 0:TT]), inc="s_st0", n=16)
                continue
            if dbg == "h0":
                for eng in ("s_vector", "s_gpsimd", "s_scalar", "s_tensor"):
                    b.wait("gpsimd", eng, b.counts[eng])
                dumps = [hsl(HC_QH, TT), hsl(HC_KH, TT), hsl(HC_VH, TT),
                         hsl(HC_GH, TT), hsl(HC_GSIL, TT),
                         hsl(HC_ONB, TT),
                         hsl(HC_LAD + L_KP8 * 1024, TT),
                         hsl(HC_LAD + L_KAP * 1024, TT),
                         hsl(HC_LAD + L_KBR * 1024, TT)]
                for di, src in enumerate(dumps):
                    b.op("gpsimd", lambda e, di=di, it=it, src=src: e.dma_start(
                        out=outT[di * P:(di + 1) * P, it * TT:(it + 1) * TT],
                        in_=src), inc="s_st0", n=16)
                b.op("gpsimd", lambda e, it=it: e.dma_start(
                    out=outT[9 * P:10 * P, it * TT:(it + 1) * TT],
                    in_=cgw[:, 0:TT]), inc="s_st0", n=16)
                b.op("gpsimd", lambda e, it=it: e.dma_start(
                    out=outT[10 * P:11 * P, it * TT:(it + 1) * TT],
                    in_=Sst[:, 0:CH].broadcast_to([P, CH]) if False else Sst[:, 0:TT//4].rearrange("p (a c) -> p (a c)") if False else Sst[:, 0:CH]),
                    inc="s_st0", n=16) if False else None
                w = b.op("gpsimd", lambda e, it=it: e.dma_start(
                    out=outT[10 * P:11 * P, it * TT:it * TT + NCHUNK * NH],
                    in_=bett[:, :]), inc="s_st0", n=16)
                continue
            onb_done = b.counts["s_gpsimd"]
            op_pe = {}
            for dt in range(NDC):
                slot = dt % 2
                buf = wA if slot == 0 else wB
                semw = "s_wa0" if slot == 0 else "s_wb0"
                if dt >= 2:
                    b.wait("sync", "s_tensor", op_pe[dt - 2])
                b.wait("sync", "s_tensor", slot_last["wa0" if slot == 0 else "wb0"])
                w = b.op("sync", lambda e, dt=dt, buf=buf: e.dma_start(
                    out=buf[:, 0:D], in_=wob[dt]), inc=semw, n=16)
                b.wait("tensor", semw, w)
                if dt == 0:
                    b.wait("tensor", "s_gpsimd", onb_done)
                pp = PS[slot]
                b.guard(f"ps{slot}", "tensor")
                for hh in range(NH):
                    nt = b.op("tensor", lambda e, hh=hh, buf=buf, pp=pp: e.matmul(
                        pp[:, :], buf[:, hh * P:(hh + 1) * P],
                        hsl(HC_ONB + hh * TT, TT),
                        start=(hh == 0), stop=(hh == NH - 1)),
                        inc="s_tensor" if hh == NH - 1 else None)
                op_pe[dt] = nt
                slot_last["wa0" if slot == 0 else "wb0"] = nt
                b.touch(f"ps{slot}", "tensor", nt)
                b.wait("vector", "s_tensor", nt)
                nv = b.op("vector", lambda e, dt=dt, pp=pp: e.tensor_add(
                    out=xres[:, dt * TT:(dt + 1) * TT], in0=pp[:, :],
                    in1=xres[:, dt * TT:(dt + 1) * TT]), inc="s_vector")
                b.touch(f"ps{slot}", "vector", nv)

            if dbg == "attn":
                b.wait("gpsimd", "s_vector", b.counts["s_vector"])
                dbg_store(it, xres, False)
                continue
            b.wait("gpsimd", "s_vector", b.counts["s_vector"])
            xs_done, xs_v = ln_pass("s_vector", b.counts["s_vector"])
            last_out_ev = ffn_pass(it, first=False, xs_v=xs_v)

        b.wait("gpsimd", "s_st0", b.counts["s_st0"])
        b.wait("gpsimd", "s_st1", b.counts["s_st1"])

        with nc.Block() as block:
            @block.sync
            def _(e):
                for fn in b.streams["sync"]:
                    fn(e)

            @block.scalar
            def _(e):
                for fn in b.streams["scalar"]:
                    fn(e)

            @block.vector
            def _(e):
                for fn in b.streams["vector"]:
                    fn(e)

            @block.tensor
            def _(e):
                for fn in b.streams["tensor"]:
                    fn(e)

            @block.gpsimd
            def _(e):
                for fn in b.streams["gpsimd"]:
                    fn(e)

    return nc


def make_inputs(core_inputs, ntile=8):
    x = np.asarray(core_inputs["L0_attn_hidden_states"])
    shared = {
        "w1tf": np.ascontiguousarray(np.asarray(core_inputs["L0_ffn_W1"]).T),
        "w3tf": np.ascontiguousarray(np.asarray(core_inputs["L0_ffn_W3"]).T),
        "w2tf": np.ascontiguousarray(np.asarray(core_inputs["L0_ffn_W2"]).T),
        "wq": np.ascontiguousarray(core_inputs["L0_attn_Wq"]),
        "wk": np.ascontiguousarray(core_inputs["L0_attn_Wk"]),
        "wv": np.ascontiguousarray(core_inputs["L0_attn_Wv"]),
        "wg": np.ascontiguousarray(core_inputs["L0_attn_Wg"]),
        "wo": np.ascontiguousarray(core_inputs["L0_attn_Wo"]),
        "wf1": np.ascontiguousarray(core_inputs["L0_attn_Wf1"]),
        "wf2": np.ascontiguousarray(core_inputs["L0_attn_Wf2"]),
        "wb": np.ascontiguousarray(core_inputs["L0_attn_Wb"]),
        "convq": np.ascontiguousarray(core_inputs["L0_attn_conv_q"]),
        "convk": np.ascontiguousarray(core_inputs["L0_attn_conv_k"]),
        "convv": np.ascontiguousarray(core_inputs["L0_attn_conv_v"]),
        "alog": np.ascontiguousarray(core_inputs["L0_attn_A_log"]),
        "dtb": np.ascontiguousarray(core_inputs["L0_attn_dt_bias"]),
        "bgv": np.ascontiguousarray(core_inputs["L0_attn_bg"]),
        "onw": np.ascontiguousarray(core_inputs["L0_attn_o_norm_w"]),
        "cst": _np_consts(),
    }
    in_maps = []
    for bi in range(x.shape[0]):
        m = dict(shared)
        m["xT"] = np.ascontiguousarray(np.asarray(x[bi]).T[:, :ntile * TT])
        in_maps.append(m)
    return in_maps


def kernel(**inputs):
    ntile = 8
    nc = build_nc(ntile)
    in_maps = make_inputs(inputs, ntile)
    res = run_bass_kernel_spmd(nc, in_maps, core_ids=list(range(8)))
    outs = [np.asarray(r["outT"]).T for r in res.results]
    return np.stack(outs).astype(np.float32)
